# version 01
# baseline (speedup 1.0000x reference)
import contextlib
import numpy as np
import concourse.bass as bass
import concourse.mybir as mybir
from concourse.bass_utils import run_bass_kernel_spmd

F32 = mybir.dt.float32
AF = mybir.ActivationFunctionType
ALU = mybir.AluOpType
AX = mybir.AxisListType

B = 2
S = 4096
D = 2048
NH = 16
TOK = 1024
E = 8
IM = 1408
ISH = 5632
DCH = D // 128
NQT = TOK // 128
ISQ = INV_SQRT_HD = 1.0 / float(np.sqrt(128.0))
LN_EPS = 1e-5

_OVR = {}


class Ctr:
    def __init__(self, nc, name):
        self.h = nc.alloc_semaphore(name=name)
        self.n = 0

    def inc(self, ins, k=1):
        ins.then_inc(self.h, k)
        self.n += k
        return self.n


class Env:
    pass


def build(dbg=(), phases=("p1", "p2", "p3", "p4", "p5")):
    nc = bass.Bass("TRN2", target_bir_lowering=False)
    env = Env()
    env.nc = nc
    t = env.t = {}
    for name, shape in (
        ("xT", [D, S]), ("xTo", [D, TOK]), ("qoff", [128, 1]),
        ("wqT", [D, D]), ("wkT", [D, D]), ("wvT", [D, D]), ("wpT", [D, D]),
        ("rwT", [D, E]), ("guT", [E, D, 2 * IM]), ("dpT", [E, IM, D]),
        ("sgT", [D, ISH]), ("suT", [D, ISH]), ("sdT", [ISH, D]), ("sgwT", [128, DCH]),
    ):
        t[name] = nc.dram_tensor(name, shape, F32, kind="ExternalInput")
    t["y"] = nc.dram_tensor("y", [D, TOK], F32, kind="ExternalOutput")
    t["ksc"] = nc.dram_tensor("ksc", [D, S], F32, kind="Internal")
    t["vsc"] = nc.dram_tensor("vsc", [S, D], F32, kind="Internal")
    t["qsc"] = nc.dram_tensor("qsc", [D, TOK], F32, kind="Internal")
    t["osc"] = nc.dram_tensor("osc", [TOK, D], F32, kind="Internal")
    t["csc"] = nc.dram_tensor("csc", [E, TOK], F32, kind="Internal")

    env.dbg = {}
    for name, shape in dbg:
        env.dbg[name] = nc.dram_tensor("dbg_" + name, shape, F32, kind="ExternalOutput")

    env.ident = nc.alloc_sbuf_tensor("ident", [128, 128], F32)
    env.onescol = nc.alloc_sbuf_tensor("onescol", [128, 1], F32)
    env.onesrow = nc.alloc_sbuf_tensor("onesrow", [1, 128], F32)
    env.p02 = contextlib.ExitStack()
    env.kpos = env.p02.enter_context(nc.sbuf_tensor("kpos", [128, S], F32))
    env.qpos = env.p02.enter_context(nc.sbuf_tensor("qpos", [128, NQT], F32))
    env.qoff_sb = env.p02.enter_context(nc.sbuf_tensor("qoff_sb", [128, 1], F32))

    env.s = {k: Ctr(nc, k) for k in
             ("lw", "lx", "pe", "pe2", "ev", "ev2", "sc", "sp", "st", "a1", "a2", "a3")}

    emit_p0(env)
    if "p1" in phases:
        emit_p1(env)
    if "p2" in phases:
        emit_p2(env)
    env.p02.close()
    if "p3" in phases or "p4" in phases or "p5" in phases:
        env.h1 = nc.alloc_sbuf_tensor("h1", [128, DCH * TOK], F32)
    if "p3" in phases:
        emit_p3(env)
    if "p4" in phases:
        env.sig = nc.alloc_sbuf_tensor("sig", [1, TOK], F32)
        emit_p4(env)
    if "p5" in phases:
        emit_p5(env)

    with nc.Block() as blk:
        @blk.sync
        def _(sync):
            sync.wait_ge(env.s["st"].h, env.s["st"].n)
            sync.wait_ge(env.s["sp"].h, env.s["sp"].n)
    return nc


def emit_p0(env):
    nc, t = env.nc, env.t
    lx = env.s["lx"]
    a1 = env.s["a1"]
    with nc.Block() as blk:
        @blk.sync
        def _(sync):
            lx.inc(sync.dma_start(env.qoff_sb[:], t["qoff"].ap()), 16)

        @blk.gpsimd
        def _(g):
            g.memset(env.onescol[:], 1.0)
            g.memset(env.onesrow[:], 1.0)
            g.iota(env.kpos[:], [[1, S]], base=0, channel_multiplier=0,
                   allow_small_or_imprecise_dtypes=True)
            a1.inc(g.iota(env.qpos[:], [[128, NQT]], base=0, channel_multiplier=1,
                          allow_small_or_imprecise_dtypes=True), 1)
            g.memset(env.ident[:], 1.0)
            g.affine_select(env.ident[:], env.ident[:], [[1, 128]], ALU.is_equal, 0.0,
                            base=0, channel_multiplier=-1)

        @blk.vector
        def _(v):
            v.wait_ge(lx.h, lx.n)
            v.wait_ge(a1.h, a1.n)
            v.tensor_scalar_add(env.qpos[:], env.qpos[:], env.qoff_sb[:, 0:1])


def nc_ap(handle):
    return handle[:]


def emit_p1(env):
    nc, t, s = env.nc, env.t, env.s
    NTT = S // 512
    jobs = []
    for tt in range(NTT):
        jobs.append(("k", tt, "xT", t["wkT"], t["ksc"]))
        jobs.append(("v", tt, "xT", t["wvT"], t["vsc"]))
    for tq in range(TOK // 512):
        jobs.append(("q", tq, "xTo", t["wqT"], t["qsc"]))

    lw, lx, pe, pe2, ev, ev2, sp = (s[k] for k in ("lw", "lx", "pe", "pe2", "ev", "ev2", "sp"))
    pe_b, pe2_b, ev_b, ev2_b, lw_b, lx_b, sp_b = pe.n, pe2.n, ev.n, ev2.n, lw.n, lx.n, sp.n

    xloads = []
    for j in jobs:
        key = (j[2], j[1])
        if key not in xloads:
            xloads.append(key)

    pe_after = lambda j, f: pe_b + 16 * j + f + 1
    ev_after = lambda j, f: ev_b + 16 * j + f + 1
    lw_after = lambda j, f: lw_b + 16 * (16 * j + f + 1)
    lx_after = lambda i: lx_b + 16 * (i + 1)
    pe2_after = lambda vj, c: pe2_b + 64 * vj + c + 1
    ev2_after = lambda vj, c: ev2_b + 64 * vj + c + 1
    spill_counts = []
    c0 = sp_b
    for job in jobs:
        c0 += 16 * (4 if job[0] == "v" else 1)
        spill_counts.append(c0)

    with (
        nc.sbuf_tensor("p1_x0", [128, DCH * 512], F32) as x0,
        nc.sbuf_tensor("p1_x1", [128, DCH * 512], F32) as x1,
        nc.sbuf_tensor("p1_w0", [128, DCH * 128], F32) as w0,
        nc.sbuf_tensor("p1_w1", [128, DCH * 128], F32) as w1,
        nc.sbuf_tensor("p1_kst", [128, DCH * 512], F32) as kst,
        nc.sbuf_tensor("p1_vst", [128, DCH * 512], F32) as vst,
        nc.sbuf_tensor("p1_vtst", [128, 4 * D], F32) as vtst,
        nc.psum_tensor("p1_ps", [128, 1024], F32) as ps,
        nc.psum_tensor("p1_pt", [128, 1024], F32) as pt,
        nc.Block() as blk,
    ):
        xbufs, wbufs = (x0, x1), (w0, w1)

        @blk.sync
        def _(sync):
            li = 0
            for j, job in enumerate(jobs):
                kind, ts, xsrc, wT, dst = job
                idx = xloads.index((xsrc, ts))
                if idx == li:
                    if idx >= 2:
                        last_j = max(jj for jj, jb in enumerate(jobs)
                                     if (jb[2], jb[1]) == xloads[idx - 2])
                        sync.wait_ge(pe.h, pe_after(last_j, 15))
                    src = t[xsrc].ap()[:, ts * 512:(ts + 1) * 512] \
                        .rearrange("(c p) n -> p c n", p=128)
                    lx.inc(sync.dma_start(
                        xbufs[idx % 2][:].rearrange("p (c n) -> p c n", c=DCH), src), 16)
                    li += 1
                for f in range(16):
                    gi = 16 * j + f
                    if gi >= 2:
                        pj, pf = divmod(gi - 2, 16)
                        sync.wait_ge(pe.h, pe_after(pj, pf))
                    src = wT.ap()[:, f * 128:(f + 1) * 128] \
                        .rearrange("(c p) n -> p c n", p=128)
                    lw.inc(sync.dma_start(
                        wbufs[gi % 2][:].rearrange("p (c n) -> p c n", c=DCH), src), 16)

        @blk.tensor
        def _(tensor):
            vj = 0
            for j, job in enumerate(jobs):
                kind, ts, xsrc, wT, dst = job
                xi = xloads.index((xsrc, ts))
                xb = xbufs[xi % 2]
                for f in range(16):
                    gi = 16 * j + f
                    tensor.wait_ge(lw.h, lw_after(j, f))
                    if f == 0:
                        tensor.wait_ge(lx.h, lx_after(xi))
                    if gi >= 2:
                        pj, pf = divmod(gi - 2, 16)
                        tensor.wait_ge(ev.h, ev_after(pj, pf))
                    wb = wbufs[gi % 2]
                    pss = ps[:, (gi % 2) * 512:(gi % 2) * 512 + 512]
                    for d in range(DCH):
                        ins = tensor.matmul(pss, wb[:, d * 128:(d + 1) * 128],
                                            xb[:, d * 512:(d + 1) * 512],
                                            start=(d == 0), stop=(d == DCH - 1))
                    pe.inc(ins, 1)
                if kind == "v":
                    for c in range(64):
                        sblk, fc = divmod(c, 16)
                        tensor.wait_ge(ev.h, ev_after(j, fc))
                        if c >= 2:
                            tensor.wait_ge(ev2.h, ev2_after(vj, c - 2))
                        ins = tensor.transpose(
                            pt[:, (c % 2) * 512:(c % 2) * 512 + 128],
                            vst[:, fc * 512 + sblk * 128: fc * 512 + sblk * 128 + 128],
                            env.ident[:])
                        pe2.inc(ins, 1)
                    vj += 1

        @blk.vector
        def _(vector):
            vj = 0
            for j, job in enumerate(jobs):
                kind = job[0]
                stg = vst if kind == "v" else kst
                prev = None
                for jj in range(j - 1, -1, -1):
                    if (jobs[jj][0] == "v") == (kind == "v"):
                        prev = jj
                        break
                for f in range(16):
                    gi = 16 * j + f
                    vector.wait_ge(pe.h, pe_after(j, f))
                    if f == 0 and prev is not None:
                        vector.wait_ge(sp.h, spill_counts[prev])
                    pss = ps[:, (gi % 2) * 512:(gi % 2) * 512 + 512]
                    ev.inc(vector.tensor_copy(stg[:, f * 512:(f + 1) * 512], pss), 1)
                if kind == "v":
                    for c in range(64):
                        sblk, fc = divmod(c, 16)
                        vector.wait_ge(pe2.h, pe2_after(vj, c))
                        ev2.inc(vector.tensor_copy(
                            vtst[:, sblk * D + fc * 128: sblk * D + fc * 128 + 128],
                            pt[:, (c % 2) * 512:(c % 2) * 512 + 128]), 1)
                    vj += 1

        @blk.gpsimd
        def _(g):
            vj = 0
            for j, job in enumerate(jobs):
                kind, ts, xsrc, wT, dst = job
                if kind in ("k", "q"):
                    g.wait_ge(ev.h, ev_after(j, 15))
                    dstap = dst.ap()[:, ts * 512:(ts + 1) * 512] \
                        .rearrange("(c p) n -> p c n", p=128)
                    sp.inc(g.dma_start(dstap, kst[:].rearrange("p (c n) -> p c n", c=DCH)), 16)
                else:
                    for sblk in range(4):
                        g.wait_ge(ev2.h, ev2_after(vj, 16 * (sblk + 1) - 1))
                        row = ts * 512 + sblk * 128
                        sp.inc(g.dma_start(dst.ap()[row:row + 128, :],
                                           vtst[:, sblk * D:(sblk + 1) * D]), 16)
                    vj += 1

    dbg_dump(env, [("ksc", t["ksc"], None), ("vsc", t["vsc"], None), ("qsc", t["qsc"], None)],
             wait=("sp",))


def dbg_dump(env, items, wait=()):
    nc, s = env.nc, env.s
    items = [(n, src, ap) for (n, src, ap) in items if n in env.dbg]
    if not items:
        return
    with nc.Block() as blk:
        @blk.sync
        def _(sync):
            for w in wait:
                sync.wait_ge(s[w].h, s[w].n)
            for name, src, ap in items:
                srcap = src.ap() if hasattr(src, "ap") else (ap if ap is not None else src[:])
                s["st"].inc(sync.dma_start(env.dbg[name].ap(), srcap), 16)


def emit_p2(env):
    nc, t, s = env.nc, env.t, env.s
    lx, pe, ev, sc, sp = (s[k] for k in ("lx", "pe", "ev", "sc", "sp"))
    lx_b, pe_b, ev_b, sc_b, sp_b = lx.n, pe.n, ev.n, sc.n, sp.n

    PE_PP = 34
    EV_PP = 36

    def pe_qk(p):
        return pe_b + PE_PP * p + 1

    def pe_tr(p, c):
        return pe_b + PE_PP * p + 2 + c

    def pe_av(p):
        return pe_b + PE_PP * p + PE_PP

    def ev_mx(p):
        return ev_b + EV_PP * p + 1

    def ev_stt(p):
        return ev_b + EV_PP * p + 2

    def ev_rc(p):
        return ev_b + EV_PP * p + 3

    def ev_et(p, c):
        return ev_b + EV_PP * p + 4 + c

    def ev_o(p):
        return ev_b + EV_PP * p + EV_PP

    def sc_exp(p):
        return sc_b + 8 * (p + 1)

    with (
        nc.sbuf_tensor("p2_k0", [128, S], F32) as k0,
        nc.sbuf_tensor("p2_k1", [128, S], F32) as k1,
        nc.sbuf_tensor("p2_v0", [128, S], F32) as v0,
        nc.sbuf_tensor("p2_v1", [128, S], F32) as v1,
        nc.sbuf_tensor("p2_q0", [128, TOK], F32) as q0,
        nc.sbuf_tensor("p2_q1", [128, TOK], F32) as q1,
        nc.sbuf_tensor("p2_e", [128, S], F32) as e_sb,
        nc.sbuf_tensor("p2_em", [128, S], F32) as e_m,
        nc.sbuf_tensor("p2_et", [128, 256], F32) as et_sb,
        nc.sbuf_tensor("p2_o", [128, 256], F32) as o_st,
        nc.sbuf_tensor("p2_bm", [128, 64], F32) as bm,
        nc.sbuf_tensor("p2_ac", [128, 64], F32) as acc,
        nc.sbuf_tensor("p2_nm", [128, 8], F32) as nm,
        nc.sbuf_tensor("p2_bi", [128, 8], F32) as bias,
        nc.sbuf_tensor("p2_ll", [128, 8], F32) as ll,
        nc.sbuf_tensor("p2_rr", [128, 8], F32) as rr,
        nc.psum_tensor("p2_ps", [128, 4096], F32) as pss,
        nc.Block() as blk,
    ):
        kb_, vb_, qb_ = (k0, k1), (v0, v1), (q0, q1)

        @blk.sync
        def _(sync):
            sync.wait_ge(sp.h, sp_b)
            for h in range(NH):
                if h >= 2:
                    sync.wait_ge(pe.h, pe_av((h - 1) * NQT - 1))
                lx.inc(sync.dma_start(kb_[h % 2][:],
                                      t["ksc"].ap()[h * 128:(h + 1) * 128, :]), 16)
                src = t["vsc"].ap()[:, h * 128:(h + 1) * 128] \
                    .rearrange("(c p) n -> p c n", p=128)
                lx.inc(sync.dma_start(
                    vb_[h % 2][:].rearrange("p (c n) -> p c n", c=S // 128), src), 16)
                lx.inc(sync.dma_start(qb_[h % 2][:],
                                      t["qsc"].ap()[h * 128:(h + 1) * 128, :]), 16)

        @blk.tensor
        def _(tensor):
            for h in range(NH):
                kb, vb, qb = kb_[h % 2], vb_[h % 2], qb_[h % 2]
                for qt in range(NQT):
                    p = h * NQT + qt
                    if qt == 0:
                        tensor.wait_ge(lx.h, lx_b + 48 * (h + 1))
                    if p > 0:
                        tensor.wait_ge(ev.h, ev_o(p - 1))
                        tensor.wait_ge(sc.h, sc_exp(p - 1))
                    for kbi in range(8):
                        ins = tensor.matmul(pss[:, kbi * 512:(kbi + 1) * 512],
                                            qb[:, qt * 128:(qt + 1) * 128],
                                            kb[:, kbi * 512:(kbi + 1) * 512],
                                            start=True, stop=True)
                    pe.inc(ins, 1)
                    for c in range(32):
                        if c == 0:
                            tensor.wait_ge(ev.h, ev_stt(p))
                        if c >= 4:
                            tensor.wait_ge(ev.h, ev_et(p, c - 4))
                        ins = tensor.transpose(
                            pss[:, (c % 4) * 512:(c % 4) * 512 + 128],
                            e_m[:, c * 128:(c + 1) * 128], env.ident[:])
                        pe.inc(ins, 1)
                        tensor.wait_ge(ev.h, ev_et(p, c))
                        ins = tensor.matmul(pss[:, 4 * 512: 4 * 512 + 128],
                                            et_sb[:, (c % 2) * 128:(c % 2) * 128 + 128],
                                            vb[:, c * 128:(c + 1) * 128],
                                            start=(c == 0), stop=(c == 31))
                        if c == 31:
                            pe.inc(ins, 1)

        @blk.vector
        def _(vector):
            NOPS = 6
            for h in range(NH):
                for qt in range(NQT):
                    p = h * NQT + qt
                    vector.wait_ge(pe.h, pe_qk(p))
                    for i in range(8):
                        vector.reduce_max(bm[:, 8 * i:8 * i + 1],
                                          pss[:, i * 512:(i + 1) * 512], axis=AX.X)
                    for _ in range(NOPS):
                        vector.engine_nop()
                    vector.reduce_max(nm[:, 0:1], bm[:, 0:57:8], axis=AX.X, negate=True)
                    for _ in range(NOPS):
                        vector.engine_nop()
                    ev.inc(vector.tensor_scalar_mul(bias[:, 0:1], nm[:, 0:1], ISQ), 1)
                    vector.wait_ge(sc.h, sc_exp(p))
                    for i in range(8):
                        ins = vector.scalar_tensor_tensor(
                            e_m[:, i * 512:(i + 1) * 512],
                            env.kpos[:, i * 512:(i + 1) * 512],
                            env.qpos[:, qt:qt + 1], e_sb[:, i * 512:(i + 1) * 512],
                            ALU.is_le, ALU.mult, accum_out=acc[:, 8 * i:8 * i + 1])
                    ev.inc(ins, 1)
                    for _ in range(NOPS):
                        vector.engine_nop()
                    vector.reduce_sum(ll[:, 0:1], acc[:, 0:57:8], axis=AX.X)
                    for _ in range(NOPS):
                        vector.engine_nop()
                    ev.inc(vector.reciprocal(rr[:, 0:1], ll[:, 0:1]), 1)
                    for c in range(32):
                        vector.wait_ge(pe.h, pe_tr(p, c))
                        ev.inc(vector.tensor_copy(
                            et_sb[:, (c % 2) * 128:(c % 2) * 128 + 128],
                            pss[:, (c % 4) * 512:(c % 4) * 512 + 128]), 1)
                    vector.wait_ge(pe.h, pe_av(p))
                    if p >= 2:
                        vector.wait_ge(sp.h, sp_b + 16 * (p - 1))
                    ev.inc(vector.tensor_scalar_mul(
                        o_st[:, (p % 2) * 128:(p % 2) * 128 + 128],
                        pss[:, 4 * 512: 4 * 512 + 128], rr[:, 0:1]), 1)

        @blk.scalar
        def _(scalar):
            for h in range(NH):
                for qt in range(NQT):
                    p = h * NQT + qt
                    scalar.wait_ge(ev.h, ev_mx(p))
                    for i in range(8):
                        sc.inc(scalar.activation(e_sb[:, i * 512:(i + 1) * 512],
                                                 pss[:, i * 512:(i + 1) * 512], AF.Exp,
                                                 bias=bias[:, 0:1], scale=ISQ), 1)

        @blk.gpsimd
        def _(g):
            for h in range(NH):
                for qt in range(NQT):
                    p = h * NQT + qt
                    g.wait_ge(ev.h, ev_o(p))
                    sp.inc(g.dma_start(
                        t["osc"].ap()[qt * 128:(qt + 1) * 128, h * 128:(h + 1) * 128],
                        o_st[:, (p % 2) * 128:(p % 2) * 128 + 128]), 16)

    dbg_dump(env, [("osc", t["osc"], None)], wait=("sp",))


def emit_ln(env, buf, stride, off, width, tag=""):
    nc, s = env.nc, env.s
    pe, ev, sc = s["pe"], s["ev"], s["sc"]
    NHW = width // 512
    assert width % 512 == 0 and NHW in (1, 2)
    pe_b, ev_b, sc_b = pe.n, ev.n, sc.n
    NOPS = 6

    with (
        nc.sbuf_tensor(f"ln_sq{tag}", [128, 2 * width], F32) as sq,
        nc.sbuf_tensor(f"ln_srow{tag}", [1, 2 * width], F32) as srow,
        nc.sbuf_tensor(f"ln_inv{tag}", [1, 2 * width], F32) as invr,
        nc.psum_tensor(f"ln_ps{tag}", [128, 2 * width], F32) as pln,
        nc.psum_tensor(f"ln_bc{tag}", [128, 2 * width], F32) as pbc,
        nc.Block() as blk,
    ):
        def sl(d):
            return buf[:, d * stride + off: d * stride + off + width]

        @blk.scalar
        def _(scalar):
            for d in range(DCH):
                if d >= 2:
                    scalar.wait_ge(pe.h, pe_b + d - 1)
                sc.inc(scalar.activation(sq[:, (d % 2) * width:(d % 2) * width + width],
                                         sl(d), AF.Square), 1)
            scalar.wait_ge(ev.h, ev_b + 2)
            sc.inc(scalar.activation(invr[0:1, width:2 * width], srow[0:1, width:2 * width],
                                     AF.Sqrt, bias=0.0, scale=1.0), 1)

        @blk.tensor
        def _(tensor):
            for d in range(DCH):
                tensor.wait_ge(sc.h, sc_b + d + 1)
                for hw in range(NHW):
                    tensor.matmul(pln[0:1, hw * 512:(hw + 1) * 512], env.onescol[:],
                                  sl(d)[:, hw * 512:(hw + 1) * 512],
                                  start=(d == 0), stop=(d == DCH - 1))
                    ins = tensor.matmul(
                        pln[0:1, width + hw * 512: width + (hw + 1) * 512],
                        env.onescol[:],
                        sq[:, (d % 2) * width + hw * 512:(d % 2) * width + (hw + 1) * 512],
                        start=(d == 0), stop=(d == DCH - 1))
                pe.inc(ins, 1)
            tensor.wait_ge(ev.h, ev_b + 4)
            for hw in range(NHW):
                tensor.matmul(pbc[:, hw * 512:(hw + 1) * 512], env.onesrow[:],
                              invr[0:1, hw * 512:(hw + 1) * 512], start=True, stop=True)
                ins = tensor.matmul(pbc[:, width + hw * 512: width + (hw + 1) * 512],
                                    env.onesrow[:],
                                    srow[0:1, width + hw * 512: width + (hw + 1) * 512],
                                    start=True, stop=True)
            pe.inc(ins, 1)

        @blk.vector
        def _(vector):
            vector.wait_ge(pe.h, pe_b + DCH)
            ev.inc(vector.tensor_scalar_mul(srow[0:1, 0:2 * width],
                                            pln[0:1, 0:2 * width], 1.0 / D), 1)
            for _ in range(NOPS):
                vector.engine_nop()
            vector.tensor_mul(invr[0:1, 0:width], srow[0:1, 0:width], srow[0:1, 0:width])
            for _ in range(NOPS):
                vector.engine_nop()
            vector.tensor_sub(invr[0:1, width:2 * width],
                              srow[0:1, width:2 * width], invr[0:1, 0:width])
            for _ in range(NOPS):
                vector.engine_nop()
            ev.inc(vector.tensor_scalar_add(srow[0:1, width:2 * width],
                                            invr[0:1, width:2 * width], LN_EPS), 1)
            vector.wait_ge(sc.h, sc_b + DCH + 1)
            vector.reciprocal(invr[0:1, 0:width], invr[0:1, width:2 * width])
            for _ in range(NOPS):
                vector.engine_nop()
            ev.inc(vector.engine_nop(), 1)
            ev.inc(vector.tensor_mul(srow[0:1, width:2 * width], srow[0:1, 0:width],
                                     invr[0:1, 0:width]), 1)
            vector.wait_ge(pe.h, pe_b + DCH + 1)
            for d in range(DCH):
                vector.tensor_mul(sq[:, 0:width], sl(d), pbc[:, 0:width])
                ins = vector.tensor_sub(sl(d), sq[:, 0:width], pbc[:, width:2 * width])
            ev.inc(ins, 1)


def emit_p3(env):
    nc, t, s = env.nc, env.t, env.s
    h1 = env.h1
    lw, lx, pe, pe2, ev, ev2, sp = (s[k] for k in ("lw", "lx", "pe", "pe2", "ev", "ev2", "sp"))
    lw_b, lx_b, pe_b, pe2_b, ev_b, ev2_b = lw.n, lx.n, pe.n, pe2.n, ev.n, ev2.n

    pe_pf = lambda tt, f: pe_b + 16 * tt + f + 1
    ev_r1 = lambda tt, f: ev_b + 16 * tt + f + 1
    pe2_tr = lambda tt, c: pe2_b + 64 * tt + c + 1
    ev2_tr = lambda tt, c: ev2_b + 64 * tt + c + 1
    lw_after = lambda gi: lw_b + 16 * (gi + 1)
    lx_after = lambda i: lx_b + 16 * (i + 1)

    with (
        nc.sbuf_tensor("p3_ot", [128, 4 * D], F32) as ot,
        nc.sbuf_tensor("p3_afm", [128, DCH * 512], F32) as afm,
        nc.sbuf_tensor("p3_w0", [128, DCH * 128], F32) as w0,
        nc.sbuf_tensor("p3_w1", [128, DCH * 128], F32) as w1,
        nc.sbuf_tensor("p3_xr", [128, 1024], F32) as xr,
        nc.psum_tensor("p3_ps", [128, 1024], F32) as ps,
        nc.psum_tensor("p3_pt", [128, 2048], F32) as pt,
        nc.Block() as blk,
    ):
        wb_ = (w0, w1)

        @blk.sync
        def _(sync):
            xl = 0
            for tt in range(2):
                sync.wait_ge(sp.h, sp.n)
                if tt == 1:
                    sync.wait_ge(pe2.h, pe2_tr(0, 63))
                for cch in range(4):
                    lx.inc(sync.dma_start(
                        ot[:, cch * D:(cch + 1) * D],
                        t["osc"].ap()[tt * 512 + cch * 128: tt * 512 + (cch + 1) * 128, :]), 16)
                    xl += 1
                for f in range(16):
                    gi = 16 * tt + f
                    if gi >= 2:
                        pj, pf = divmod(gi - 2, 16)
                        sync.wait_ge(pe.h, pe_pf(pj, pf))
                    src = t["wpT"].ap()[:, f * 128:(f + 1) * 128] \
                        .rearrange("(c p) n -> p c n", p=128)
                    lw.inc(sync.dma_start(
                        wb_[gi % 2][:].rearrange("p (c n) -> p c n", c=DCH), src), 16)
                    if gi >= 2:
                        pj, pf = divmod(gi - 2, 16)
                        sync.wait_ge(ev.h, ev_r1(pj, pf))
                    src2 = t["xTo"].ap()[f * 128:(f + 1) * 128, tt * 512:(tt + 1) * 512]
                    lx.inc(sync.dma_start(xr[:, (gi % 2) * 512:(gi % 2) * 512 + 512], src2), 16)
                    xl += 1

        @blk.tensor
        def _(tensor):
            for tt in range(2):
                for c in range(64):
                    sblk, fc = divmod(c, 16)
                    tensor.wait_ge(lx.h, lx_after(tt * 20 + sblk))
                    if c >= 4:
                        tensor.wait_ge(ev2.h, ev2_tr(tt, c - 4))
                    ins = tensor.transpose(pt[:, (c % 4) * 512:(c % 4) * 512 + 128],
                                           ot[:, sblk * D + fc * 128: sblk * D + fc * 128 + 128],
                                           env.ident[:])
                    pe2.inc(ins, 1)
                for f in range(16):
                    gi = 16 * tt + f
                    tensor.wait_ge(lw.h, lw_after(gi))
                    if f == 0:
                        tensor.wait_ge(ev2.h, ev2_tr(tt, 63))
                    if gi >= 2:
                        pj, pf = divmod(gi - 2, 16)
                        tensor.wait_ge(ev.h, ev_r1(pj, pf))
                    for d in range(DCH):
                        ins = tensor.matmul(ps[:, (gi % 2) * 512:(gi % 2) * 512 + 512],
                                            wb_[gi % 2][:, d * 128:(d + 1) * 128],
                                            afm[:, d * 512:(d + 1) * 512],
                                            start=(d == 0), stop=(d == DCH - 1))
                    pe.inc(ins, 1)

        @blk.vector
        def _(vector):
            for tt in range(2):
                for c in range(64):
                    sblk, fc = divmod(c, 16)
                    vector.wait_ge(pe2.h, pe2_tr(tt, c))
                    if tt == 1 and c == 0:
                        vector.wait_ge(pe.h, pe_pf(0, 15))
                    ev2.inc(vector.tensor_copy(
                        afm[:, fc * 512 + sblk * 128: fc * 512 + sblk * 128 + 128],
                        pt[:, (c % 4) * 512:(c % 4) * 512 + 128]), 1)
                for f in range(16):
                    gi = 16 * tt + f
                    vector.wait_ge(pe.h, pe_pf(tt, f))
                    vector.wait_ge(lx.h, lx_after(tt * 20 + 4 + f))
                    ev.inc(vector.tensor_add(
                        h1[:, f * TOK + tt * 512: f * TOK + tt * 512 + 512],
                        ps[:, (gi % 2) * 512:(gi % 2) * 512 + 512],
                        xr[:, (gi % 2) * 512:(gi % 2) * 512 + 512]), 1)

    emit_ln(env, h1, TOK, 0, 512, "a")
    emit_ln(env, h1, TOK, 512, 512, "b")
    if "h1" in env.dbg:
        with nc.Block() as blk:
            @blk.sync
            def _(sync):
                for c2 in range(DCH):
                    s["st"].inc(sync.dma_start(
                        env.dbg["h1"].ap()[c2 * 128:(c2 + 1) * 128, :],
                        h1[:, c2 * TOK:(c2 + 1) * TOK]), 16)


def emit_p4(env):
    nc, t, s = env.nc, env.t, env.s
    h1 = env.h1
    lw, pe, ev, sc, sp = (s[k] for k in ("lw", "pe", "ev", "sc", "sp"))
    NOPS = 6

    with contextlib.ExitStack() as stack:
        en = stack.enter_context
        rw_sb = en(nc.sbuf_tensor("p4_rw", [128, DCH * E], F32))
        zw_sb = en(nc.sbuf_tensor("p4_zw", [128, DCH], F32))
        lg = en(nc.sbuf_tensor("p4_lg", [E, TOK], F32))
        lt = en(nc.sbuf_tensor("p4_lt", [128, 8], F32))
        er = en(nc.sbuf_tensor("p4_er", [128, 8], F32))
        pr = en(nc.sbuf_tensor("p4_pr", [128, 8], F32))
        eq1 = en(nc.sbuf_tensor("p4_e1", [128, 8], F32))
        t1 = en(nc.sbuf_tensor("p4_t1", [128, 8], F32))
        pr2 = en(nc.sbuf_tensor("p4_p2", [128, 8], F32))
        eq2 = en(nc.sbuf_tensor("p4_e2", [128, 8], F32))
        t2 = en(nc.sbuf_tensor("p4_t2", [128, 8], F32))
        cm = en(nc.sbuf_tensor("p4_cm", [128, 8], F32))
        cmb = en(nc.sbuf_tensor("p4_cmb", [E, TOK], F32))
        s_nm = en(nc.sbuf_tensor("p4_s0", [128, 8], F32))
        s_sum = en(nc.sbuf_tensor("p4_s1", [128, 8], F32))
        s_rs = en(nc.sbuf_tensor("p4_s2", [128, 8], F32))
        s_m1 = en(nc.sbuf_tensor("p4_s3", [128, 8], F32))
        s_m2 = en(nc.sbuf_tensor("p4_s4", [128, 8], F32))
        s_w = en(nc.sbuf_tensor("p4_s5", [128, 8], F32))
        s_wr = en(nc.sbuf_tensor("p4_s6", [128, 8], F32))
        pl = en(nc.psum_tensor("p4_pl", [128, 1024], F32))
        pz = en(nc.psum_tensor("p4_pz", [128, 1024], F32))
        pt = en(nc.psum_tensor("p4_pt", [128, 1024], F32))
        blk = en(nc.Block())
        lw_b, pe_b, ev_b, sc_b, sp_b = lw.n, pe.n, ev.n, sc.n, sp.n
        pe_t = lambda i, k: pe_b + 2 + 2 * i + k
        ev_t = lambda i, k: ev_b + 1 + 3 * i + k
        sc_t = lambda i: sc_b + 1 + i + 1

        @blk.sync
        def _(sync):
            src2 = t["rwT"].ap()[:, :].rearrange("(c p) n -> p c n", p=128)
            lw.inc(sync.dma_start(rw_sb[:].rearrange("p (c n) -> p c n", c=DCH), src2), 16)
            lw.inc(sync.dma_start(zw_sb[:], t["sgwT"].ap()), 16)

        @blk.tensor
        def _(tensor):
            tensor.wait_ge(lw.h, lw_b + 32)
            for hw in range(2):
                for d in range(DCH):
                    tensor.matmul(pl[0:E, hw * 512:(hw + 1) * 512],
                                  rw_sb[:, d * E:(d + 1) * E],
                                  h1[:, d * TOK + hw * 512: d * TOK + hw * 512 + 512],
                                  start=(d == 0), stop=(d == DCH - 1))
                    ins = tensor.matmul(pz[0:1, hw * 512:(hw + 1) * 512],
                                        zw_sb[:, d:d + 1],
                                        h1[:, d * TOK + hw * 512: d * TOK + hw * 512 + 512],
                                        start=(d == 0), stop=(d == DCH - 1))
                pe.inc(ins, 1)
            for i in range(NQT):
                par = i % 2
                tensor.wait_ge(ev.h, ev_b + 1)
                if i >= 2:
                    tensor.wait_ge(ev.h, ev_t(i - 2, 3))
                ins = tensor.transpose(pt[:, par * 512: par * 512 + E],
                                       lg[0:E, i * 128:(i + 1) * 128],
                                       env.ident[0:E, 0:E])
                pe.inc(ins, 1)
                tensor.wait_ge(ev.h, ev_t(i, 2))
                ins = tensor.transpose(pt[0:E, par * 512 + 128: par * 512 + 256],
                                       cm[:, 0:E], env.ident[:])
                pe.inc(ins, 1)

        @blk.vector
        def _(vector):
            vector.wait_ge(pe.h, pe_b + 2)
            vector.tensor_copy(lg[0:E, 0:512], pl[0:E, 0:512])
            ev.inc(vector.tensor_copy(lg[0:E, 512:1024], pl[0:E, 512:1024]), 1)
            for i in range(NQT):
                par = i % 2
                vector.wait_ge(pe.h, pe_t(i, 1))
                vector.tensor_copy(lt[:, 0:E], pt[:, par * 512: par * 512 + E])
                for _ in range(NOPS):
                    vector.engine_nop()
                ev.inc(vector.reduce_max(s_nm[:, 0:1], lt[:, 0:E], axis=AX.X, negate=True), 1)
                vector.wait_ge(sc.h, sc_t(i))
                vector.reciprocal(s_rs[:, 0:1], s_sum[:, 0:1])
                for _ in range(NOPS):
                    vector.engine_nop()
                vector.tensor_scalar_mul(pr[:, 0:E], er[:, 0:E], s_rs[:, 0:1])
                for _ in range(NOPS):
                    vector.engine_nop()
                vector.reduce_max(s_m1[:, 0:1], pr[:, 0:E], axis=AX.X)
                for _ in range(NOPS):
                    vector.engine_nop()
                vector.tensor_scalar(eq1[:, 0:E], pr[:, 0:E], s_m1[:, 0:1], None, ALU.is_equal)
                for _ in range(NOPS):
                    vector.engine_nop()
                vector.tensor_mul(t1[:, 0:E], pr[:, 0:E], eq1[:, 0:E])
                for _ in range(NOPS):
                    vector.engine_nop()
                vector.tensor_sub(pr2[:, 0:E], pr[:, 0:E], t1[:, 0:E])
                for _ in range(NOPS):
                    vector.engine_nop()
                vector.reduce_max(s_m2[:, 0:1], pr2[:, 0:E], axis=AX.X)
                for _ in range(NOPS):
                    vector.engine_nop()
                vector.tensor_scalar(eq2[:, 0:E], pr2[:, 0:E], s_m2[:, 0:1], None, ALU.is_equal)
                for _ in range(NOPS):
                    vector.engine_nop()
                vector.tensor_add(t1[:, 0:E], eq1[:, 0:E], eq2[:, 0:E])
                vector.tensor_add(s_w[:, 0:1], s_m1[:, 0:1], s_m2[:, 0:1])
                for _ in range(NOPS):
                    vector.engine_nop()
                vector.reciprocal(s_wr[:, 0:1], s_w[:, 0:1])
                vector.tensor_mul(t2[:, 0:E], pr[:, 0:E], t1[:, 0:E])
                for _ in range(NOPS):
                    vector.engine_nop()
                ev.inc(vector.tensor_scalar_mul(cm[:, 0:E], t2[:, 0:E], s_wr[:, 0:1]), 1)
                vector.wait_ge(pe.h, pe_t(i, 2))
                ev.inc(vector.tensor_copy(cmb[0:E, i * 128:(i + 1) * 128],
                                          pt[0:E, par * 512 + 128: par * 512 + 256]), 1)

        @blk.scalar
        def _(scalar):
            scalar.wait_ge(pe.h, pe_b + 2)
            scalar.wait_ge(ev.h, ev_b + 1)
            scalar.activation(env.sig[0:1, 0:512], pz[0:1, 0:512], AF.Sigmoid)
            sc.inc(scalar.activation(env.sig[0:1, 512:1024], pz[0:1, 512:1024], AF.Sigmoid), 1)
            for i in range(NQT):
                scalar.wait_ge(ev.h, ev_t(i, 1))
                sc.inc(scalar.activation(er[:, 0:E], lt[:, 0:E], AF.Exp,
                                         bias=s_nm[:, 0:1], scale=1.0,
                                         accum_out=s_sum[:, 0:1]), 1)

        @blk.gpsimd
        def _(g):
            g.wait_ge(ev.h, ev_t(NQT - 1, 3))
            sp.inc(g.dma_start(t["csc"].ap(), cmb[0:E, :]), 16)

    if "cmb" in env.dbg or "sig" in env.dbg:
        with nc.Block() as blk:
            @blk.sync
            def _(sync):
                sync.wait_ge(sp.h, sp.n)
                if "cmb" in env.dbg:
                    s["st"].inc(sync.dma_start(env.dbg["cmb"].ap(), t["csc"].ap()), 16)
                if "sig" in env.dbg:
                    s["st"].inc(sync.dma_start(env.dbg["sig"].ap(), env.sig[0:1, :]), 16)


def emit_p5(env):
    nc, t, s = env.nc, env.t, env.s
    h1 = env.h1
    lw, lx, pe, ev, sc, sp, st = (s[k] for k in ("lw", "lx", "pe", "ev", "sc", "sp", "st"))

    moe = nc.alloc_sbuf_tensor("p5_moe", [128, DCH * 512], F32)
    sig_bc = nc.alloc_sbuf_tensor("p5_sigbc", [128, TOK], F32)
    NSH = ISH // 128
    NIM = IM // 128

    with (
        nc.psum_tensor("p5_pb", [128, 1024], F32) as pb,
        nc.Block() as blk,
    ):
        pe_b, ev_b = pe.n, ev.n

        @blk.sync
        def _(sync):
            sync.wait_ge(sp.h, sp.n)

        @blk.tensor
        def _(tensor):
            for hw in range(2):
                ins = tensor.matmul(pb[:, hw * 512:(hw + 1) * 512], env.onesrow[:],
                                    env.sig[0:1, hw * 512:(hw + 1) * 512],
                                    start=True, stop=True)
            pe.inc(ins, 1)

        @blk.vector
        def _(vector):
            vector.wait_ge(pe.h, pe_b + 1)
            for hw in range(2):
                ev.inc(vector.tensor_copy(sig_bc[:, hw * 512:(hw + 1) * 512],
                                          pb[:, hw * 512:(hw + 1) * 512]), 1)

    for H in range(2):
        _p5_shared(env, moe, sig_bc, H)
        _p5_experts(env, moe, H)
        _p5_tail(env, moe, H)


def _p5_shared(env, moe, sig_bc, H):
    nc, t, s = env.nc, env.t, env.s
    h1 = env.h1
    lw, pe, ev, sc = (s[k] for k in ("lw", "pe", "ev", "sc"))
    NSH = ISH // 128

    for q in range(2):
        off = H * 512 + q * 256
        with contextlib.ExitStack() as ostack:
            act = ostack.enter_context(nc.sbuf_tensor(f"p5s_act{H}{q}", [128, NSH * 256], F32))
            stack = ostack.enter_context(contextlib.ExitStack())
            en = stack.enter_context
            g0 = en(nc.sbuf_tensor(f"p5s_g0{H}{q}", [128, DCH * 128], F32))
            g1 = en(nc.sbuf_tensor(f"p5s_g1{H}{q}", [128, DCH * 128], F32))
            u0 = en(nc.sbuf_tensor(f"p5s_u0{H}{q}", [128, DCH * 128], F32))
            u1 = en(nc.sbuf_tensor(f"p5s_u1{H}{q}", [128, DCH * 128], F32))
            tmp = en(nc.sbuf_tensor(f"p5s_tmp{H}{q}", [128, 1024], F32))
            tm2 = en(nc.sbuf_tensor(f"p5s_tm2{H}{q}", [128, 1024], F32))
            pg = en(nc.psum_tensor(f"p5s_pg{H}{q}", [128, 1024], F32))
            pu = en(nc.psum_tensor(f"p5s_pu{H}{q}", [128, 1024], F32))
            blk = en(nc.Block())
            gb, ub = (g0, g1), (u0, u1)
            lw_b, pe_b, ev_b, sc_b = lw.n, pe.n, ev.n, sc.n

            @blk.sync
            def _(sync):
                for i in range(NSH):
                    if i >= 2:
                        sync.wait_ge(pe.h, pe_b + i - 1)
                    src = t["sgT"].ap()[:, i * 128:(i + 1) * 128] \
                        .rearrange("(c p) n -> p c n", p=128)
                    lw.inc(sync.dma_start(
                        gb[i % 2][:].rearrange("p (c n) -> p c n", c=DCH), src), 16)
                    src = t["suT"].ap()[:, i * 128:(i + 1) * 128] \
                        .rearrange("(c p) n -> p c n", p=128)
                    lw.inc(sync.dma_start(
                        ub[i % 2][:].rearrange("p (c n) -> p c n", c=DCH), src), 16)

            @blk.tensor
            def _(tensor):
                for i in range(NSH):
                    tensor.wait_ge(lw.h, lw_b + 32 * (i + 1))
                    if i >= 2:
                        tensor.wait_ge(sc.h, sc_b + i - 1)
                        tensor.wait_ge(ev.h, ev_b + i - 1)
                    par = i % 2
                    for d in range(DCH):
                        tensor.matmul(pg[:, par * 512: par * 512 + 256],
                                      gb[par][:, d * 128:(d + 1) * 128],
                                      h1[:, d * TOK + off: d * TOK + off + 256],
                                      start=(d == 0), stop=(d == DCH - 1))
                        ins = tensor.matmul(pu[:, par * 512: par * 512 + 256],
                                            ub[par][:, d * 128:(d + 1) * 128],
                                            h1[:, d * TOK + off: d * TOK + off + 256],
                                            start=(d == 0), stop=(d == DCH - 1))
                    pe.inc(ins, 1)

            @blk.scalar
            def _(scalar):
                for i in range(NSH):
                    par = i % 2
                    scalar.wait_ge(pe.h, pe_b + i + 1)
                    sc.inc(scalar.activation(tmp[:, par * 512: par * 512 + 256],
                                             pg[:, par * 512: par * 512 + 256],
                                             AF.Silu), 1)

            @blk.vector
            def _(vector):
                for i in range(NSH):
                    par = i % 2
                    vector.wait_ge(sc.h, sc_b + i + 1)
                    vector.tensor_mul(tm2[:, par * 512: par * 512 + 256],
                                      tmp[:, par * 512: par * 512 + 256],
                                      sig_bc[:, off:off + 256])
                    for _ in range(4):
                        vector.engine_nop()
                    ev.inc(vector.tensor_mul(act[:, i * 256:(i + 1) * 256],
                                             tm2[:, par * 512: par * 512 + 256],
                                             pu[:, par * 512: par * 512 + 256]), 1)

            stack.close()
            _p5_shared_down(env, moe, act, H, q)


def _p5_shared_down(env, moe, act, H, q):
    nc, t, s = env.nc, env.t, env.s
    lw, pe, ev = (s[k] for k in ("lw", "pe", "ev"))
    NSH = ISH // 128
    with contextlib.ExitStack() as stack:
        en = stack.enter_context
        s0 = en(nc.sbuf_tensor(f"p5d_s0{H}{q}", [128, NSH * 128], F32))
        s1 = en(nc.sbuf_tensor(f"p5d_s1{H}{q}", [128, NSH * 128], F32))
        pd = en(nc.psum_tensor(f"p5d_pd{H}{q}", [128, 1024], F32))
        blk = en(nc.Block())
        sb = (s0, s1)
        lw_b, pe_b, ev_b = lw.n, pe.n, ev.n

        @blk.sync
        def _(sync):
            for f in range(DCH):
                if f >= 2:
                    sync.wait_ge(pe.h, pe_b + f - 1)
                src = t["sdT"].ap()[:, f * 128:(f + 1) * 128] \
                    .rearrange("(c p) n -> p c n", p=128)
                lw.inc(sync.dma_start(
                    sb[f % 2][:].rearrange("p (c n) -> p c n", c=NSH), src), 16)

        @blk.tensor
        def _(tensor):
            for f in range(DCH):
                tensor.wait_ge(lw.h, lw_b + 16 * (f + 1))
                if f >= 2:
                    tensor.wait_ge(ev.h, ev_b + f - 1)
                par = f % 2
                for i in range(NSH):
                    ins = tensor.matmul(pd[:, par * 512: par * 512 + 256],
                                        sb[par][:, i * 128:(i + 1) * 128],
                                        act[:, i * 256:(i + 1) * 256],
                                        start=(i == 0), stop=(i == NSH - 1))
                pe.inc(ins, 1)

        @blk.vector
        def _(vector):
            for f in range(DCH):
                par = f % 2
                vector.wait_ge(pe.h, pe_b + f + 1)
                ev.inc(vector.tensor_copy(
                    moe[:, f * 512 + q * 256: f * 512 + q * 256 + 256],
                    pd[:, par * 512: par * 512 + 256]), 1)


def _p5_experts(env, moe, H):
    nc, t, s = env.nc, env.t, env.s
    h1 = env.h1
    lw, lx, pe, ev, sc = (s[k] for k in ("lw", "lx", "pe", "ev", "sc"))
    NIM = IM // 128
    off = H * 512

    with contextlib.ExitStack() as stack:
        en = stack.enter_context
        he = en(nc.sbuf_tensor(f"p5e_he{H}", [128, NIM * 512], F32))
        g0 = en(nc.sbuf_tensor(f"p5e_g0{H}", [128, DCH * 128], F32))
        g1 = en(nc.sbuf_tensor(f"p5e_g1{H}", [128, DCH * 128], F32))
        u0 = en(nc.sbuf_tensor(f"p5e_u0{H}", [128, DCH * 128], F32))
        u1 = en(nc.sbuf_tensor(f"p5e_u1{H}", [128, DCH * 128], F32))
        d0 = en(nc.sbuf_tensor(f"p5e_d0{H}", [128, NIM * 128], F32))
        d1 = en(nc.sbuf_tensor(f"p5e_d1{H}", [128, NIM * 128], F32))
        bc = en(nc.sbuf_tensor(f"p5e_bc{H}", [128, 512], F32))
        cmbr = en(nc.sbuf_tensor(f"p5e_cr{H}", [1, 2 * 512], F32))
        tmp = en(nc.sbuf_tensor(f"p5e_tmp{H}", [128, 1024], F32))
        tm2 = en(nc.sbuf_tensor(f"p5e_tm2{H}", [128, 1024], F32))
        pg = en(nc.psum_tensor(f"p5e_pg{H}", [128, 1024], F32))
        pu = en(nc.psum_tensor(f"p5e_pu{H}", [128, 1024], F32))
        pd = en(nc.psum_tensor(f"p5e_pd{H}", [128, 1024], F32))
        pb = en(nc.psum_tensor(f"p5e_pb{H}", [128, 512], F32))
        blk = en(nc.Block())
        gb, ub, db = (g0, g1), (u0, u1), (d0, d1)
        lw_b, pe_b, ev_b, sc_b, lx_bE = lw.n, pe.n, ev.n, sc.n, lx.n
        PE_E = 1 + NIM + DCH
        EV_E = 1 + NIM + DCH
        SC_E = NIM
        LW_E = 16 * (2 * NIM + DCH)

        pe_bc = lambda e: pe_b + PE_E * e + 1
        pe_gu = lambda e, i: pe_b + PE_E * e + 1 + i + 1
        pe_dn = lambda e, f: pe_b + PE_E * e + 1 + NIM + f + 1
        ev_bc = lambda e: ev_b + EV_E * e + 1
        ev_he = lambda e, i: ev_b + EV_E * e + 1 + i + 1
        ev_mo = lambda e, f: ev_b + EV_E * e + 1 + NIM + f + 1
        sc_si = lambda e, i: sc_b + SC_E * e + i + 1
        lw_gu = lambda e, i: lw_b + LW_E * e + 32 * (i + 1)
        lw_dp = lambda e, f: lw_b + LW_E * e + 32 * NIM + 16 * (f + 1)

        @blk.sync
        def _(sync):
            for e in range(E):
                if e >= 2:
                    sync.wait_ge(pe.h, pe_bc(e - 1))
                lx.inc(sync.dma_start(
                    cmbr[0:1, (e % 2) * 512:(e % 2) * 512 + 512],
                    t["csc"].ap()[e:e + 1, off:off + 512]), 16)
                for i in range(NIM):
                    if (e, i) != (0, 0) and (e * NIM + i) >= 2:
                        pj = (e * NIM + i) - 2
                        sync.wait_ge(pe.h, pe_gu(pj // NIM, pj % NIM))
                    src = t["guT"].ap()[e, :, i * 128:(i + 1) * 128] \
                        .rearrange("(c p) n -> p c n", p=128)
                    lw.inc(sync.dma_start(
                        gb[i % 2][:].rearrange("p (c n) -> p c n", c=DCH), src), 16)
                    src = t["guT"].ap()[e, :, IM + i * 128: IM + (i + 1) * 128] \
                        .rearrange("(c p) n -> p c n", p=128)
                    lw.inc(sync.dma_start(
                        ub[i % 2][:].rearrange("p (c n) -> p c n", c=DCH), src), 16)
                for f in range(DCH):
                    if (e * DCH + f) >= 2:
                        pj = (e * DCH + f) - 2
                        sync.wait_ge(pe.h, pe_dn(pj // DCH, pj % DCH))
                    src = t["dpT"].ap()[e, :, f * 128:(f + 1) * 128] \
                        .rearrange("(c p) n -> p c n", p=128)
                    lw.inc(sync.dma_start(
                        db[f % 2][:].rearrange("p (c n) -> p c n", c=NIM), src), 16)

        @blk.tensor
        def _(tensor):
            for e in range(E):
                if e >= 1:
                    tensor.wait_ge(ev.h, ev_bc(e - 1) + NIM)
                tensor.wait_ge(lx.h, lx_bE + 16 * (e + 1))
                ins = tensor.matmul(pb[:, 0:512], env.onesrow[:],
                                    cmbr[0:1, (e % 2) * 512:(e % 2) * 512 + 512],
                                    start=True, stop=True)
                pe.inc(ins, 1)
                for i in range(NIM):
                    tensor.wait_ge(lw.h, lw_gu(e, i))
                    gi = e * NIM + i
                    if gi >= 2:
                        pj = gi - 2
                        tensor.wait_ge(sc.h, sc_si(pj // NIM, pj % NIM))
                        tensor.wait_ge(ev.h, ev_he(pj // NIM, pj % NIM))
                    par = i % 2
                    for d in range(DCH):
                        tensor.matmul(pg[:, par * 512: par * 512 + 512],
                                      gb[par][:, d * 128:(d + 1) * 128],
                                      h1[:, d * TOK + off: d * TOK + off + 512],
                                      start=(d == 0), stop=(d == DCH - 1))
                        ins = tensor.matmul(pu[:, par * 512: par * 512 + 512],
                                            ub[par][:, d * 128:(d + 1) * 128],
                                            h1[:, d * TOK + off: d * TOK + off + 512],
                                            start=(d == 0), stop=(d == DCH - 1))
                    pe.inc(ins, 1)
                for f in range(DCH):
                    tensor.wait_ge(lw.h, lw_dp(e, f))
                    gi = e * DCH + f
                    if gi >= 2:
                        pj = gi - 2
                        tensor.wait_ge(ev.h, ev_mo(pj // DCH, pj % DCH))
                    par = f % 2
                    for i in range(NIM):
                        ins = tensor.matmul(pd[:, par * 512: par * 512 + 512],
                                            db[par][:, i * 128:(i + 1) * 128],
                                            he[:, i * 512:(i + 1) * 512],
                                            start=(i == 0), stop=(i == NIM - 1))
                    pe.inc(ins, 1)

        @blk.scalar
        def _(scalar):
            for e in range(E):
                for i in range(NIM):
                    par = i % 2
                    scalar.wait_ge(pe.h, pe_gu(e, i))
                    sc.inc(scalar.activation(tmp[:, par * 512: par * 512 + 512],
                                             pg[:, par * 512: par * 512 + 512],
                                             AF.Silu), 1)

        @blk.vector
        def _(vector):
            for e in range(E):
                vector.wait_ge(pe.h, pe_bc(e))
                ev.inc(vector.tensor_copy(bc[:], pb[:, 0:512]), 1)
                for i in range(NIM):
                    par = i % 2
                    vector.wait_ge(sc.h, sc_si(e, i))
                    vector.tensor_mul(tm2[:, par * 512: par * 512 + 512],
                                      tmp[:, par * 512: par * 512 + 512],
                                      bc[:])
                    for _ in range(4):
                        vector.engine_nop()
                    ev.inc(vector.tensor_mul(he[:, i * 512:(i + 1) * 512],
                                             tm2[:, par * 512: par * 512 + 512],
                                             pu[:, par * 512: par * 512 + 512]), 1)
                for f in range(DCH):
                    par = f % 2
                    vector.wait_ge(pe.h, pe_dn(e, f))
                    ev.inc(vector.tensor_add(moe[:, f * 512:(f + 1) * 512],
                                             moe[:, f * 512:(f + 1) * 512],
                                             pd[:, par * 512: par * 512 + 512]), 1)


def _p5_tail(env, moe, H):
    nc, t, s = env.nc, env.t, env.s
    h1 = env.h1
    pe, ev, st = s["pe"], s["ev"], s["st"]

    with nc.Block() as blk:
        ev_b = ev.n

        @blk.vector
        def _(vector):
            for f in range(DCH):
                ins = vector.tensor_add(moe[:, f * 512:(f + 1) * 512],
                                        moe[:, f * 512:(f + 1) * 512],
                                        h1[:, f * TOK + H * 512: f * TOK + H * 512 + 512])
            ev.inc(ins, 1)

    emit_ln(env, moe, 512, 0, 512, f"t{H}")

    with nc.Block() as blk:
        @blk.sync
        def _(sync):
            dst = t["y"].ap()[:, H * 512:(H + 1) * 512].rearrange("(c p) n -> p c n", p=128)
            st.inc(sync.dma_start(dst, moe[:].rearrange("p (c n) -> p c n", c=DCH)), 16)


def kernel(**inputs):
    x = np.asarray(inputs["L0_attn_x"], dtype=np.float32)
    wq = np.asarray(inputs["L0_attn_Wq"], dtype=np.float32)
    wk = np.asarray(inputs["L0_attn_Wk"], dtype=np.float32)
    wv = np.asarray(inputs["L0_attn_Wv"], dtype=np.float32)
    wp = np.asarray(inputs["L0_attn_Wp"], dtype=np.float32)
    rw = np.asarray(inputs["L0_ffn_router_weight"], dtype=np.float32)
    gu = np.asarray(inputs["L0_ffn_gate_up_proj"], dtype=np.float32)
    dp = np.asarray(inputs["L0_ffn_down_proj"], dtype=np.float32)
    sg = np.asarray(inputs["L0_ffn_sh_gate_w"], dtype=np.float32)
    su = np.asarray(inputs["L0_ffn_sh_up_w"], dtype=np.float32)
    sd = np.asarray(inputs["L0_ffn_sh_down_w"], dtype=np.float32)
    sgw = np.asarray(inputs["L0_ffn_shared_gate_w"], dtype=np.float32)

    xT = [np.ascontiguousarray(x[b].T) for b in range(B)]
    common = {
        "wqT": np.ascontiguousarray(wq.T), "wkT": np.ascontiguousarray(wk.T),
        "wvT": np.ascontiguousarray(wv.T), "wpT": np.ascontiguousarray(wp.T),
        "rwT": np.ascontiguousarray(rw.T),
        "guT": np.ascontiguousarray(gu.transpose(0, 2, 1)),
        "dpT": np.ascontiguousarray(dp.transpose(0, 2, 1)),
        "sgT": np.ascontiguousarray(sg.T), "suT": np.ascontiguousarray(su.T),
        "sdT": np.ascontiguousarray(sd.T),
        "sgwT": np.ascontiguousarray(sgw.reshape(DCH, 128).T),
    }
    in_maps = []
    for c in range(8):
        b, k = c // 4, c % 4
        m = dict(common)
        m["xT"] = xT[b]
        m["xTo"] = np.ascontiguousarray(xT[b][:, k * TOK:(k + 1) * TOK])
        m["qoff"] = np.full((128, 1), k * TOK, dtype=np.float32)
        in_maps.append(m)

    nc = build(dbg=_OVR.get("dbg", ()), phases=_OVR.get("phases", ("p1", "p2", "p3", "p4", "p5")))
    res = run_bass_kernel_spmd(nc, in_maps, core_ids=list(range(8)), **_OVR.get("runkw", {}))
    if "post" in _OVR:
        return _OVR["post"](res)
    out = np.empty((B, S, D), dtype=np.float32)
    for c in range(8):
        b, k = c // 4, c % 4
        out[b, k * TOK:(k + 1) * TOK, :] = res.results[c]["y"].T
    return out
